# version 04
# baseline (speedup 1.0000x reference)
import numpy as np
from contextlib import ExitStack

import concourse.bass as bass
import concourse.mybir as mybir
from concourse.bass_utils import run_bass_kernel_spmd

F32 = mybir.dt.float32
BF16 = mybir.dt.bfloat16
AF = mybir.ActivationFunctionType
ALU = mybir.AluOpType
ts = bass.ts

D = 1024
NCH = 8
ST = 256
RT = 512
AT = 128
HBS = 512
LN_EPS = 1e-5
RMS_EPS = 1e-6
LN1E7 = float(np.log(np.float32(1e-7)))
NEG_BIG = -1e30

ALL_PARTS = ("moe1a", "attn0", "moe1b", "moe2a", "attn1", "moe2b")

DBG_SYNC = 0
DBG_DUMP = False

ENGS = (mybir.EngineType.PE, mybir.EngineType.DVE, mybir.EngineType.Activation,
        mybir.EngineType.Pool, mybir.EngineType.SP)


def _free_reg_ids(nc, eng):
    hs = []
    try:
        for i in range(64):
            hs.append(nc.alloc_register(eng, f"pr_{nc.next_id()}"))
    except ValueError:
        pass
    ids = {nc.lookup_reg(h).reg_id for h in hs}
    for h in hs:
        nc.free_register(h)
    return ids


class RegScope:

    def __init__(self, nc):
        self.nc = nc

    def __enter__(self):
        self.before = {e: _free_reg_ids(self.nc, e) for e in ENGS}
        return self

    def __exit__(self, *a):
        for e in ENGS:
            leaked = self.before[e] - _free_reg_ids(self.nc, e)
            for rid in leaked:
                h = self.nc.add_register(e, f"fr_{self.nc.next_id()}", rid)
                self.nc.free_register(h)
        return False



def _f(a):
    return np.ascontiguousarray(a, dtype=np.float32)


def pack_kchunks(W):
    d_in, d_out = W.shape
    return _f(W.reshape(d_in // 128, 128, d_out).transpose(1, 0, 2).reshape(128, -1))


def pack_cols(v):
    return _f(np.asarray(v).reshape(NCH, 128).T)


def pack_moe(WU, WG, WD):
    NE, d_in, H = WU.shape
    HB = H // HBS
    wu = WU.reshape(NE, NCH, 128, HB, HBS).transpose(0, 3, 2, 1, 4).reshape(NE, HB, 128, NCH * HBS)
    wg = WG.reshape(NE, NCH, 128, HB, HBS).transpose(0, 3, 2, 1, 4).reshape(NE, HB, 128, NCH * HBS)
    wd = WD.reshape(NE, HB, HBS // 128, 128, D).transpose(0, 1, 3, 2, 4).reshape(NE, HB, 128, (HBS // 128) * D)
    return _f(wu), _f(wg), _f(wd)


def pack_attn0_stream(Wdecay, Wgate, Wk, M_T, Wmem, Wc):
    pieces = []

    def add(P):
        cols = P.shape[1]
        n = (cols + 1023) // 1024
        Q = np.zeros((128, n * 1024), np.float32)
        Q[:, :cols] = P
        for i in range(n):
            pieces.append(Q[:, i * 1024:(i + 1) * 1024])

    for h in range(2):
        add(pack_kchunks(Wdecay[:, h * 512:(h + 1) * 512]))
    for h in range(2):
        add(pack_kchunks(Wgate[:, h * 512:(h + 1) * 512]))
    add(pack_kchunks(Wk))
    add(pack_kchunks(M_T))
    add(pack_kchunks(Wmem))
    add(pack_kchunks(Wc))
    return _f(np.stack(pieces))


def pack_attn1_stream(Wg, Wv, Wd):
    pieces = []
    for W in (Wg, Wv, Wd):
        for h in range(2):
            P = pack_kchunks(W[:, h * 512:(h + 1) * 512])
            for i in range(4):
                pieces.append(P[:, i * 1024:(i + 1) * 1024])
    return _f(np.stack(pieces))



class C:
    NOCLEAR = {"s_boot", "s_cc", "s_cc2"}

    def __init__(self, nc, ctx):
        self.nc = nc
        self.ctx = ctx
        self.s = {}
        self.n = {}

    def E(self, g):
        return self.ctx.enter_context(g)

    def sems(self, *names):
        for nm in names:
            self.s[nm] = self.E(self.nc.semaphore(nm))
            self.n[nm] = 0

    def inc(self, nm, k=1):
        self.n[nm] += k
        return self.s[nm], k

    def w(self, eng, nm, val=None):
        eng.wait_ge(self.s[nm], self.n[nm] if val is None else val)

    def soft(self, bit=None):
        if bit is not None and not (DBG_SYNC & bit):
            return
        nc = self.nc
        for eng in (nc.tensor, nc.vector, nc.scalar, nc.gpsimd, nc.sync):
            for nm, v in self.n.items():
                if v:
                    eng.wait_ge(self.s[nm], v)

    def end_body(self):
        nc = self.nc
        used = [nm for nm, v in self.n.items() if v]
        for nm in used:
            nc.sync.wait_ge(self.s[nm], self.n[nm])
        nc.all_engine_barrier()
        cleared = [nm for nm in used if nm not in self.NOCLEAR]
        for nm in cleared:
            nc.sync.sem_clear(self.s[nm])
        nc.all_engine_barrier()
        for nm in cleared:
            self.n[nm] = 0


class EngWrap:

    _PASS = {"wait_ge", "sem_clear", "dma_start", "dma_start_transpose",
             "collective_compute", "nop", "engine_nop", "Fori"}

    def __init__(self, c, eng, sem_name):
        self._c = c
        self._e = eng
        self._s = sem_name

    def __getattr__(self, op):
        f = getattr(self._e, op)
        if op in EngWrap._PASS or op.startswith("_"):
            return f

        def g(*a, **k):
            c = self._c
            c.w(self._e, self._s)
            ins = f(*a, **k)
            ins.then_inc(*c.inc(self._s))
            return ins
        return g


class Build:
    def __init__(self, nc, ctx, tok=4096, n_pairs=4):
        self.nc = nc
        self.c = C(nc, ctx)
        self.tok = tok
        self.grp = min(1024, tok)
        self.ngrp = tok // self.grp
        self.n_pairs = n_pairs
        c, E = self.c, self.c.E
        g = self.grp

        self.inputs = {}
        self.x_in = self.din("x_in", [NCH * 128, tok])
        self.x_out = nc.dram_tensor("x_out", [NCH * 128, tok], F32, kind="ExternalOutput").ap()

        self.x = E(nc.sbuf_tensor("x_res", [128, NCH * tok], F32))
        self.xn = E(nc.sbuf_tensor("xn_cache", [128, NCH * g], BF16))
        self.pf = E(nc.sbuf_tensor("pool_f", [128, 6888], F32))
        self.pb = E(nc.sbuf_tensor("pool_b", [128, 13312 + 2 * g], BF16))
        self.ones128 = E(nc.sbuf_tensor("ones128", [128, 128], BF16))
        self.ones_bf = self.ones128[0:1, :]
        self.onec_bf = self.ones128[:, 0:1]
        self.ident = None
        self.zcol = E(nc.sbuf_tensor("zcol", [128, 1], F32))
        self.ones8 = E(nc.sbuf_tensor("ones8", [128, 8], F32))
        self.cst = E(nc.sbuf_tensor("cst", [1, 8], F32))
        self.mask = None
        self.smalls = {}

        pf = self.pf
        self.xtile = pf[:, 0:4096]
        self.rows = pf[0:1, 4096:4096 + 4 * RT]
        self.tmpRT = pf[:, 6144:6656]
        self.wstage = pf[:, 0:1024]
        self.silu_t = pf[:, 1024:1536]
        self.bcast = pf[:, 1536:2048]
        self.sel_f = pf[0:8, 2048:2176]
        self.bias_u = pf[:, 2176:2184]
        self.ewt_f = pf[0:8, 1680:1808]
        self.ewt_f2 = pf[0:8, 1808:1936]
        self.a_xin = pf[:, 0:1024]
        self.a_h = pf[:, 1024:2096]
        self.a_lc = pf[:, 2096:3128]
        self.a_D = pf[:, 3128:4152]
        self.a_A = pf[:, 4152:5176]
        self.wstage2 = pf[:, 5176:6200]
        self.arows = pf[0:1, 6200:6712]
        self.a_carry = pf[:, 6712:6744]
        self.a_ofs = pf[:, 6744:6768]
        self.a_tail = pf[:, 6768:6816]
        self.gath = pf[:, 6816:6880]
        self.gath2 = pf[:, 6880:6888]

        pb = self.pb
        self.wu_sb = pb[:, 0:4096]
        self.wg_sb = pb[:, 4096:8192]
        self.wd_sb = pb[:, 8192:12288]
        self.hpr = pb[:, 12288:13312]
        self.hpr2 = E(nc.sbuf_tensor("hpr2", [128, 1024], BF16))
        self.xb_bf = pb[:, 4096:5120]
        self.xq_bf = pb[:, 5120:6144]
        self.xlo_bf = pb[:, 8192:9216]
        self.ewt_hi = pb[0:8, 13312:13312 + g]
        self.ewt_lo = pb[0:8, 13312 + g:13312 + 2 * g]
        self.sel_b = E(nc.sbuf_tensor("sel_b", [8, 128], BF16))
        self.wslot = pb[:, 0:4096]
        self.wc_ring = pb[:, 4096:6144]
        self.a_hb = pb[:, 6144:7168]
        self.a_cat = pb[:, 7168:10240]
        self.kx_bf = pb[:, 10240:10496]
        self.mm_bf = pb[:, 10496:10752]
        self.axb_bf = pb[:, 10752:11776]
        self.axq_bf = pb[:, 11776:12800]

        self.ps_a = E(nc.psum_tensor("ps_a", [128, 512], F32))
        self.ps_b = E(nc.psum_tensor("ps_b", [128, 512], F32))
        self.ps_c = E(nc.psum_tensor("ps_c", [128, 512], F32))
        self.ps_d = E(nc.psum_tensor("ps_d", [128, 512], F32))
        self.ps_big = E(nc.psum_tensor("ps_big", [128, 2048], F32))

        c.sems("s_w", "s_x", "s_mm", "s_act", "s_dve", "s_pool", "s_cc", "s_cc2", "s_boot")
        self.V = EngWrap(c, nc.vector, "s_dve")
        self.A = EngWrap(c, nc.scalar, "s_act")
        self.P = EngWrap(c, nc.gpsimd, "s_pool")

        self.cc1_in = nc.dram_tensor("cc1_in", [128, 64], F32)
        self.cc1_out = nc.dram_tensor("cc1_out", [256, 64], F32)
        self.cc2_in = nc.dram_tensor("cc2_in", [128, 8], F32)
        self.cc2_out = nc.dram_tensor("cc2_out", [256, 8], F32)

    def din(self, name, shape):
        t = self.nc.dram_tensor(name, list(shape), F32, kind="ExternalInput").ap()
        self.inputs[name] = tuple(shape)
        return t

    def small_const(self, name, shape, dtype=F32):
        t = self.din(name, shape)
        sb = self.c.E(self.nc.sbuf_tensor(name + "_sb", list(shape), dtype))
        if dtype == F32:
            self.nc.sync.dma_start(out=sb[:], in_=t).then_inc(*self.c.inc("s_x", 16))
        else:
            self.nc.gpsimd.dma_start(out=sb[:], in_=t).then_inc(*self.c.inc("s_boot", 16))
        self.smalls[name] = sb
        return sb

    def x3(self, g_sv, off, n):
        return self.x.rearrange("p (k t) -> p k t", k=NCH)[:, :, ts(g_sv, self.grp)][:, :, off:off + n]

    def x3t(self, g_sv, t_sv, n):
        return self.x.rearrange("p (k t) -> p k t", k=NCH)[:, :, ts(g_sv, self.grp)][:, :, ts(t_sv, n)]

    def x3a(self, a_sv, n):
        return self.x.rearrange("p (k t) -> p k t", k=NCH)[:, :, ts(a_sv, n)]

    @staticmethod
    def d3(ap2d, n):
        return ap2d.rearrange("p (k t) -> p k t", k=NCH)

    @staticmethod
    def bc3(row2d, n):
        return row2d.rearrange("p (o t) -> p o t", o=1).broadcast_to([128, NCH, n])

    def prologue(self):
        nc, c = self.nc, self.c
        nc.sync.dma_start(out=self.x.rearrange("p (k t) -> p k t", k=NCH),
                          in_=self.x_in.rearrange("(k p) t -> p k t", p=128)).then_inc(*c.inc("s_x", 16))
        self.V.memset(self.pf[:], 0.0)
        self.V.memset(self.pb[:], 0.0)
        self.V.memset(self.xn[:], 0.0)
        self.V.memset(self.zcol[:], 0.0)
        self.V.memset(self.ones128[:], 1.0)
        self.V.memset(self.cst[:, 0:1], LN_EPS)
        self.V.memset(self.cst[:, 1:2], RMS_EPS)
        self.V.memset(self.cst[:, 2:8], 0.0)
        self.V.memset(self.ones8[:], 1.0)
        self.mask = self.small_const("mask_in", [128, 1])
        self.ident = self.small_const("ident_in", [128, 128])

    def prologue_finish(self):
        nc, c = self.nc, self.c
        c.w(nc.vector, "s_x")
        self.c.end_body()

    def epilogue(self):
        nc, c = self.nc, self.c
        nc.sync.dma_start(out=self.x_out.rearrange("(k p) t -> p k t", p=128),
                          in_=self.x.rearrange("p (k t) -> p k t", k=NCH)).then_inc(*c.inc("s_x", 16))
        c.w(nc.sync, "s_x")
        nc.all_engine_barrier()
        for nm in c.s:
            nc.sync.sem_clear(c.s[nm])
        nc.all_engine_barrier()

    def stats(self, src_fn, n, rms, eps_col, rows_f, rows_b, xb, xq, router=None):
        nc, c = self.nc, self.c
        nq = n // 128
        for q in range(nq):
            mmv = c.n["s_mm"]
            for k in range(NCH):
                src = src_fn(k)[:, q * 128:(q + 1) * 128]
                if q > 0:
                    c.w(nc.vector, "s_mm", mmv)
                    c.w(nc.scalar, "s_mm", mmv)
                self.V.tensor_copy(xb[:, k * 128:(k + 1) * 128], src)
                self.A.activation(xq[:, k * 128:(k + 1) * 128], src, AF.Square)
                if router is not None:
                    t = self.tmpRT[:, 0:128]
                    self.V.tensor_copy(t, xb[:, k * 128:(k + 1) * 128])
                    self.V.tensor_tensor(t, src, t, ALU.subtract)
                    self.V.tensor_copy(self.xlo_bf[:, k * 128:(k + 1) * 128], t)
            c.w(nc.tensor, "s_dve")
            c.w(nc.tensor, "s_act")
            if not rms:
                for k in range(NCH):
                    nc.tensor.matmul(self.ps_a[0:1, q * 128:(q + 1) * 128], self.onec_bf[:],
                                     xb[:, k * 128:(k + 1) * 128], start=(k == 0), stop=(k == NCH - 1))
            for k in range(NCH):
                mm = nc.tensor.matmul(self.ps_b[0:1, q * 128:(q + 1) * 128], self.onec_bf[:],
                                      xq[:, k * 128:(k + 1) * 128], start=(k == 0), stop=(k == NCH - 1))
            if router is not None:
                wrh, wrl = router
                raw = self.ps_big[:, q * 8:(q + 1) * 8]
                for k in range(NCH):
                    nc.tensor.matmul(raw, xb[:, k * 128:(k + 1) * 128], wrh[:, k * 8:(k + 1) * 8],
                                     start=(k == 0), stop=False)
                    nc.tensor.matmul(raw, xb[:, k * 128:(k + 1) * 128], wrl[:, k * 8:(k + 1) * 8],
                                     start=False, stop=False)
                    mm = nc.tensor.matmul(raw, self.xlo_bf[:, k * 128:(k + 1) * 128],
                                          wrh[:, k * 8:(k + 1) * 8],
                                          start=False, stop=(k == NCH - 1))
            mm.then_inc(*c.inc("s_mm"))
        c.w(nc.vector, "s_mm")
        inv = 1.0 / float(D)
        Rm = rows_f[0:1, 0:n]
        Rv = rows_f[0:1, n:2 * n]
        Rr = rows_f[0:1, 2 * n:3 * n]
        Rs = rows_f[0:1, 3 * n:4 * n]
        Bm = rows_b[0:1, 0:n]
        Bh = rows_b[0:1, n:2 * n]
        Bl = rows_b[0:1, 2 * n:3 * n]
        if rms:
            self.V.memset(Bm, 0.0)
            self.V.tensor_scalar(Rv, self.ps_b[0:1, 0:n], inv, None, ALU.mult)
        else:
            self.V.tensor_scalar(Rm, self.ps_a[0:1, 0:n], inv, None, ALU.mult)
            self.V.tensor_copy(Bm, Rm)
            self.V.tensor_scalar(Rv, self.ps_b[0:1, 0:n], inv, None, ALU.mult)
            self.V.tensor_tensor(Rs, Rm, Rm, ALU.mult)
            self.V.tensor_tensor(Rv, Rv, Rs, ALU.subtract)
        c.w(nc.scalar, "s_dve")
        self.A.activation(Rr, Rv, AF.Ln, bias=eps_col)
        self.A.activation(Rr, Rr, AF.Exp, scale=-0.5)
        c.w(nc.vector, "s_act")
        self.V.tensor_copy(Bh, Rr)
        self.V.tensor_copy(Rs, Bh)
        self.V.tensor_tensor(Rs, Rr, Rs, ALU.subtract)
        self.V.tensor_copy(Bl, Rs)

    def stat_bcast(self, n, rows_b, mean_too=True):
        nc, c = self.nc, self.c
        c.w(nc.tensor, "s_dve")
        if mean_too:
            nc.tensor.matmul(self.ps_c[:, 0:n], self.ones_bf[0:1, :], rows_b[0:1, 0:n],
                             start=True, stop=True)
        nc.tensor.matmul(self.ps_d[:, 0:n], self.ones_bf[0:1, :], rows_b[0:1, n:2 * n],
                         start=True, stop=False)
        nc.tensor.matmul(self.ps_d[:, 0:n], self.ones_bf[0:1, :], rows_b[0:1, 2 * n:3 * n],
                         start=False, stop=True).then_inc(*c.inc("s_mm"))
        c.w(nc.vector, "s_mm")

    def moe_routing(self, g_sv, kind, tag_wr, brc, bnc, bl2_sb):
        nc, c = self.nc, self.c
        grp = self.grp
        n_rt = grp // RT if grp >= RT else 1
        rt_n = min(RT, grp)
        wrh = self.smalls[tag_wr + "_wrh"]
        wrl = self.smalls[tag_wr + "_wrl"]
        csb = self.smalls[tag_wr + "_csb"]
        for rt in range(n_rt):
            off = rt * rt_n
            c.w(nc.vector, "s_mm")
            self.V.tensor_copy(self.d3(self.xtile[:, 0:NCH * rt_n], rt_n),
                                  self.x3(g_sv, off, rt_n))
            c.w(nc.scalar, "s_dve")
            src = lambda k: self.xtile[:, k * rt_n:(k + 1) * rt_n]
            self.stats(src, rt_n, False, self.cst[0:1, 0:1], self.rows, self.rows2_b(),
                       self.xb_bf, self.xq_bf, router=(wrh, wrl))
            self.stat_bcast(rt_n, self.rows2_b())
            c.soft(1)
            self.V.tensor_scalar(self.rows[0:1, 3 * rt_n:4 * rt_n], self.rows[0:1, 0:rt_n],
                                    -1.0, None, ALU.mult)
            xt3 = self.d3(self.xtile[:, 0:NCH * rt_n], rt_n)
            self.V.tensor_tensor(xt3, xt3, self.bc3(self.ps_c[:, 0:rt_n], rt_n), ALU.subtract)
            self.V.tensor_tensor(self.d3(self.xn[:, 0:NCH * grp], grp)[:, :, off:off + rt_n],
                                    xt3, self.bc3(self.ps_d[:, 0:rt_n], rt_n),
                                    ALU.mult)
            c.soft(4)
            for q in range(rt_n // 128):
                tm = off + q * 128
                c.w(nc.tensor, "s_dve")
                nc.tensor.transpose(self.ps_a[:, 0:1],
                                    self.rows[0:1, 3 * rt_n + q * 128:3 * rt_n + q * 128 + 128],
                                    self.ident[0:1, 0:1])
                nc.tensor.transpose(self.ps_a[:, 8:9],
                                    self.rows[0:1, 2 * rt_n + q * 128:2 * rt_n + q * 128 + 128],
                                    self.ident[0:1, 0:1]).then_inc(*c.inc("s_mm"))
                c.w(nc.vector, "s_mm")
                nmc = self.rscr(7)[:, 0:1]
                rsc = self.rscr(7)[:, 1:2]
                self.V.tensor_copy(nmc, self.ps_a[:, 0:1])
                self.V.tensor_copy(rsc, self.ps_a[:, 8:9])
                lg = self.rscr(8)
                self.V.scalar_tensor_tensor(lg, csb[:], nmc, self.ps_big[:, q * 8:(q + 1) * 8],
                                               ALU.mult, ALU.add)
                self.V.tensor_scalar(lg, lg, rsc, None, ALU.mult)
                self.route_math(kind, brc, bnc)
                if DBG_DUMP and rt == 0 and q == 0:
                    if not hasattr(self, "dbg_rm"):
                        self.dbg_rm = c.E(nc.sbuf_tensor("dbg_rm", [128, 96], F32))
                        self.V.memset(self.dbg_rm[:], 0.0)
                    self.V.tensor_copy(self.dbg_rm[:, 0:80], self.pf[:, 6656:6736])
                    self.V.tensor_copy(self.dbg_rm[:, 80:88], self.smalls[tag_wr + "_csb"][:])
                    self.V.tensor_copy(self.dbg_rm[:, 88:96], self.ps_big[:, 0:8])
                c.w(nc.tensor, "s_dve")
                nc.tensor.transpose(self.ps_b[0:8, 0:128], self.ew_ap(), self.ident[:]).then_inc(*c.inc("s_mm"))
                c.w(nc.vector, "s_mm")
                self.V.tensor_copy(self.ewt_f[:], self.ps_b[0:8, 0:128])
                self.V.tensor_copy(self.ewt_hi[:, tm:tm + 128], self.ewt_f[:])
                self.V.tensor_copy(self.ewt_f2[:], self.ewt_hi[:, tm:tm + 128])
                self.V.tensor_tensor(self.ewt_f2[:], self.ewt_f[:], self.ewt_f2[:], ALU.subtract)
                self.V.tensor_copy(self.ewt_lo[:, tm:tm + 128], self.ewt_f2[:])
                c.soft(2)
            if kind == "l1":
                c.w(nc.tensor, "s_dve")
                for hhalf in range(rt_n // 256):
                    ho = off + hhalf * 256
                    for m in range(NCH):
                        nc.tensor.matmul(self.ps_big[:, m * 256:(m + 1) * 256],
                                         bl2_sb[0:8, m * 128:(m + 1) * 128],
                                         self.ewt_hi[:, ho:ho + 256], start=True, stop=False)
                        mm = nc.tensor.matmul(self.ps_big[:, m * 256:(m + 1) * 256],
                                              bl2_sb[0:8, m * 128:(m + 1) * 128],
                                              self.ewt_lo[:, ho:ho + 256], start=False, stop=True)
                    mm.then_inc(*c.inc("s_mm"))
                    c.w(nc.vector, "s_mm")
                    x3v = self.x3(g_sv, ho, 256)
                    self.V.scalar_tensor_tensor(
                        x3v, self.ps_big.rearrange("p (m t) -> p m t", m=NCH), 0.5, x3v,
                        ALU.mult, ALU.add)
                    c.w(nc.tensor, "s_dve")

    def rows2_b(self):
        return self.pb[0:1, 0:4 * RT]

    def ew_ap(self):
        return self.pf[:, 6656:6664]

    def rscr(self, i):
        return self.pf[:, 6664 + i * 8:6672 + i * 8]

    def route_math(self, kind, brc, bnc):
        nc, c = self.nc, self.c
        ew = self.ew_ap()
        s = self.rscr(0)
        e = self.rscr(1)
        sc = self.rscr(2)
        c1 = self.rscr(3)[:, 0:1]
        c2 = self.rscr(3)[:, 1:2]
        c3 = self.rscr(3)[:, 2:3]
        c4 = self.rscr(3)[:, 3:4]
        sb2 = self.rscr(4)
        sel1 = self.rscr(5)
        sel2 = self.rscr(6)
        lgin = self.rscr(8)
        if kind == "l0":
            self.V.tensor_copy(s, lgin)
            self.V.tensor_reduce(c1, s, mybir.AxisListType.X, ALU.max,
                                    negate=True)
            c.w(nc.scalar, "s_dve")
            self.A.activation(e, s, AF.Exp, bias=c1)
            c.w(nc.vector, "s_act")
            self.V.reduce_sum(c2, e, mybir.AxisListType.X)
            self.V.reciprocal(c3, c2)
            self.V.tensor_scalar(c1, c1, -1.0, None, ALU.mult)
            self.V.tensor_scalar(sc, s, c1, None, ALU.is_equal)
            self.V.tensor_tensor_scan(sb2, self.ones8[:], sc, 0.0, ALU.mult, ALU.add)
            self.V.tensor_scalar(sb2, sb2, 1.0, None, ALU.is_equal)
            self.V.tensor_tensor(sel1, sc, sb2, ALU.mult)
            self.V.tensor_scalar(c3, c3, 2.0, None, ALU.mult)
            self.V.memset(ew[:, 0:1], 1.0)
            self.V.tensor_scalar(ew[:, 1:8], sel1[:, 0:7], c3, None, ALU.mult)
        else:
            self.V.tensor_tensor(s, lgin, brc[:], ALU.add)
            self.V.tensor_tensor(sb2, s, bnc[:], ALU.add)
            self.V.tensor_reduce(c1, s, mybir.AxisListType.X, ALU.max,
                                    negate=True)
            c.w(nc.scalar, "s_dve")
            self.A.activation(e, s, AF.Exp, bias=c1)
            self.V.tensor_reduce(c2, sb2, mybir.AxisListType.X, ALU.max)
            self.V.tensor_scalar(sc, sb2, c2, None, ALU.is_equal)
            self.V.tensor_tensor_scan(sel1, self.ones8[:], sc, 0.0, ALU.mult, ALU.add)
            self.V.tensor_scalar(sel1, sel1, 1.0, None, ALU.is_equal)
            self.V.tensor_tensor(sel1, sc, sel1, ALU.mult)
            self.V.scalar_tensor_tensor(sb2, sel1, NEG_BIG, sb2, ALU.mult, ALU.add)
            self.V.tensor_reduce(c2, sb2, mybir.AxisListType.X, ALU.max)
            self.V.tensor_scalar(sc, sb2, c2, None, ALU.is_equal)
            self.V.tensor_tensor_scan(sel2, self.ones8[:], sc, 0.0, ALU.mult, ALU.add)
            self.V.tensor_scalar(sel2, sel2, 1.0, None, ALU.is_equal)
            self.V.tensor_tensor(sel2, sc, sel2, ALU.mult)
            c.w(nc.vector, "s_act")
            self.V.tensor_tensor(sc, e, sel1, ALU.mult)
            self.V.reduce_sum(c1, sc, mybir.AxisListType.X)
            self.V.tensor_tensor(sc, e, sel2, ALU.mult)
            self.V.reduce_sum(c2, sc, mybir.AxisListType.X)
            self.V.tensor_tensor(c3, c1, c2, ALU.add)
            self.V.reciprocal(c4, c3)
            self.V.tensor_scalar(sc, sel1, c1, None, ALU.mult)
            self.V.scalar_tensor_tensor(sc, sel2, c2, sc, ALU.mult, ALU.add)
            self.V.tensor_scalar(ew[:], sc, c4, None, ALU.mult)

    def moe_units(self, g_sv, wpk, selp, blup, NE, HB, out_scale, routed, l1bias):
        nc, c = self.nc, self.c
        grp = self.grp
        nst = grp // ST
        with nc.Fori(0, NE) as e_sv:
            with nc.Fori(0, HB) as hb_sv:
                units = [t[e_sv][hb_sv] for t in wpk]
                if routed:
                    nc.scalar.dma_start(out=self.sel_f[:], in_=selp[e_sv]).then_inc(*c.inc("s_x", 16))
                    c.w(nc.gpsimd, "s_x", 16)
                    self.P.tensor_copy(self.sel_b[:], self.sel_f[:])
                if l1bias:
                    nc.scalar.dma_start(out=self.bias_u[:], in_=blup[e_sv][hb_sv]).then_inc(*c.inc("s_x", 16))
                dests = []
                for j in range(8):
                    dests.append(self.wu_sb[:, j * 512:(j + 1) * 512])
                for j in range(8):
                    dests.append(self.wg_sb[:, j * 512:(j + 1) * 512])
                for j in range(8):
                    dests.append(self.wd_sb[:, j * 512:(j + 1) * 512])
                base0 = c.n["s_pool"]
                for j in range(24):
                    slot = self.wstage[:, (j % 2) * 512:(j % 2) * 512 + 512]
                    if j >= 2:
                        c.w(nc.sync, "s_pool", base0 + j - 1)
                    src = units[j // 8][:, (j % 8) * 512:(j % 8 + 1) * 512]
                    nc.sync.dma_start(out=slot, in_=src).then_inc(*c.inc("s_w", 16))
                    c.w(nc.gpsimd, "s_w", 16 * (j + 1))
                    self.P.tensor_copy(dests[j], slot)
                v_wu = base0 + 8
                v_wg = base0 + 16
                v_wd = base0 + 24
                v_evac = 0
                v_t1 = {}
                npair = nst // 2 if nst >= 2 else 1
                spw = min(512, grp)
                for sp in range(npair):
                    so = sp * spw
                    if routed:
                        c.w(nc.tensor, "s_pool", base0)
                        if v_evac:
                            c.w(nc.tensor, "s_dve", v_evac)
                        nc.tensor.matmul(self.ps_big[:, 0:spw], self.sel_b[:],
                                         self.ewt_hi[:, so:so + spw], start=True, stop=False)
                        nc.tensor.matmul(self.ps_big[:, 0:spw], self.sel_b[:],
                                         self.ewt_lo[:, so:so + spw], start=False, stop=True).then_inc(*c.inc("s_mm"))
                        c.w(nc.vector, "s_mm")
                        self.V.tensor_copy(self.bcast[:, 0:spw], self.ps_big[:, 0:spw])
                        v_bc = c.n["s_dve"]
                    for hc in range(4):
                        pu = self.ps_a if hc % 2 == 0 else self.ps_c
                        pg = self.ps_b if hc % 2 == 0 else self.ps_d
                        c.w(nc.tensor, "s_pool", v_wu if hc == 0 else v_wg)
                        pass
                        c.w(nc.tensor, "s_pool", v_wg)
                        if (sp, hc - 2) in v_t1:
                            c.w(nc.tensor, "s_dve", v_t1[(sp, hc - 2)])
                        if sp > 0 and (sp - 1, hc + 2) in v_t1:
                            c.w(nc.tensor, "s_dve", v_t1[(sp - 1, hc + 2)])
                        for k in range(NCH):
                            nc.tensor.matmul(pu[:, 0:spw],
                                             self.wu_sb[:, k * 512 + hc * 128:k * 512 + hc * 128 + 128],
                                             self.xn[:, k * grp + so:k * grp + so + spw],
                                             start=(k == 0), stop=(k == NCH - 1))
                        for k in range(NCH):
                            mm = nc.tensor.matmul(pg[:, 0:spw],
                                                  self.wg_sb[:, k * 512 + hc * 128:k * 512 + hc * 128 + 128],
                                                  self.xn[:, k * grp + so:k * grp + so + spw],
                                                  start=(k == 0), stop=(k == NCH - 1))
                        mm.then_inc(*c.inc("s_mm"))
                        v_g = c.n["s_mm"]
                        c.w(nc.scalar, "s_mm", v_g)
                        if (sp, hc - 1) in v_t1:
                            c.w(nc.scalar, "s_dve", v_t1[(sp, hc - 1)])
                        bcol = self.bias_u[:, hc:hc + 1] if l1bias else self.zcol[:]
                        if l1bias:
                            c.w(nc.scalar, "s_x", 32)
                        self.A.activation(self.silu_t[:, 0:spw], pg[:, 0:spw], AF.Silu, bias=bcol)
                        c.w(nc.vector, "s_act")
                        bcol2 = self.bias_u[:, 4 + hc:5 + hc] if l1bias else self.zcol[:]
                        self.V.scalar_tensor_tensor(self.silu_t[:, 0:spw], pu[:, 0:spw], bcol2,
                                                       self.silu_t[:, 0:spw], ALU.add, ALU.mult)
                        hp = (self.hpr[:, hc * spw:(hc + 1) * spw] if hc < 2 else
                              self.hpr2[:, (hc - 2) * spw:(hc - 1) * spw])
                        if routed:
                            self.V.tensor_tensor(hp, self.silu_t[:, 0:spw], self.bcast[:, 0:spw], ALU.mult)
                        else:
                            self.V.tensor_copy(hp, self.silu_t[:, 0:spw])
                        v_t1[(sp, hc)] = c.n["s_dve"]
                    c.w(nc.tensor, "s_pool", v_wd)
                    c.w(nc.tensor, "s_dve", v_t1[(sp, 3)])
                    for half in range(spw // ST):
                        ho = half * ST
                        if v_evac:
                            c.w(nc.tensor, "s_dve", v_evac)
                        for m in range(NCH):
                            for hc in range(4):
                                hp = (self.hpr[:, hc * spw + ho:hc * spw + ho + ST] if hc < 2 else
                                      self.hpr2[:, (hc - 2) * spw + ho:(hc - 2) * spw + ho + ST])
                                mm = nc.tensor.matmul(self.ps_big[:, m * ST:(m + 1) * ST],
                                                      self.wd_sb[:, hc * 1024 + m * 128:hc * 1024 + m * 128 + 128],
                                                      hp, start=(hc == 0), stop=(hc == 3))
                        mm.then_inc(*c.inc("s_mm"))
                        c.w(nc.vector, "s_mm")
                        x3v = self.x3(g_sv, so + ho, ST)
                        self.V.scalar_tensor_tensor(
                            x3v, self.ps_big.rearrange("p (m t) -> p m t", m=NCH), out_scale, x3v,
                            ALU.mult, ALU.add)
                        v_evac = c.n["s_dve"]
                c.end_body()

    def moe_decl(self, tag, kind, NE, HB):
        if not hasattr(self, "moe_w"):
            self.moe_w = {}
        self.moe_w[tag] = (
            (self.din(tag + "_wu", [NE, HB, 128, 4096]),
             self.din(tag + "_wg", [NE, HB, 128, 4096]),
             self.din(tag + "_wd", [NE, HB, 128, 4096])),
            self.din(tag + "_selp", [NE, 8, 128]),
            self.din(tag + "_blup", [NE, HB, 128, 8]) if kind == "l1" else None,
        )

    def moe_run(self, tag, kind, NE, HB):
        nc, c = self.nc, self.c
        wpk, selp, blup = self.moe_w[tag]
        brc = self.smalls[tag + "_brc"][:] if kind == "l1" else None
        bnc = self.smalls[tag + "_bnc"][:] if kind == "l1" else None
        bl2 = self.smalls[tag + "_bl2"][:] if kind == "l1" else None
        with RegScope(nc):
            with nc.Fori(0, self.ngrp) as g_sv:
                self.moe_routing(g_sv, kind, tag, brc, bnc, bl2)
                c.end_body()
                if DBG_DUMP:
                    dmp_e = nc.dram_tensor("dbg_ewt", [8, 2 * self.grp], F32, kind="ExternalOutput").ap()
                    dmp_x = nc.dram_tensor("dbg_xn", [128, NCH * self.grp], F32, kind="ExternalOutput").ap()
                    dscr = c.E(nc.sbuf_tensor("dbg_sb", [8, 2 * self.grp], F32))
                    dscr2 = c.E(nc.sbuf_tensor("dbg_sb2", [128, NCH * self.grp], F32))
                    self.V.tensor_copy(dscr[0:8, 0:self.grp], self.ewt_hi[:])
                    self.V.tensor_copy(dscr[0:8, self.grp:], self.ewt_lo[:])
                    dmp_r = nc.dram_tensor("dbg_rm_o", [128, 96], F32, kind="ExternalOutput").ap()
                    self.V.tensor_copy(dscr2[:], self.xn[:])
                    c.w(nc.sync, "s_dve")
                    nc.sync.dma_start(out=dmp_r, in_=self.dbg_rm[:]).then_inc(*c.inc("s_x", 16))
                    c.w(nc.sync, "s_dve")
                    nc.sync.dma_start(out=dmp_e, in_=dscr[:]).then_inc(*c.inc("s_x", 16))
                    nc.sync.dma_start(out=dmp_x, in_=dscr2[:]).then_inc(*c.inc("s_x", 16))
                    c.end_body()
                self.moe_units(g_sv, wpk, selp, blup, NE, HB, 0.5, True, kind == "l1")

    def moe_decl_smalls(self, tag, kind):
        nc, c = self.nc, self.c
        self.small_const(tag + "_wrf", [128, 64])
        if kind == "l1":
            self.small_const(tag + "_brc", [128, 8])
            self.small_const(tag + "_bnc", [128, 8])
            self.small_const(tag + "_bl2", [8, 1024], BF16)

    def moe_router_prep(self, tag):
        nc, c = self.nc, self.c
        E = c.E
        wrf = self.smalls[tag + "_wrf"]
        hi = E(nc.sbuf_tensor(tag + "_wrh", [128, 64], BF16))
        lo = E(nc.sbuf_tensor(tag + "_wrl", [128, 64], BF16))
        hif = self.pf[:, 0:64]
        csb = E(nc.sbuf_tensor(tag + "_csb", [128, 8], F32))
        csrow = self.pb[0:1, 0:8]
        self.smalls[tag + "_wrh"] = hi
        self.smalls[tag + "_wrl"] = lo
        self.smalls[tag + "_csb"] = csb
        self.V.tensor_copy(hi[:], wrf[:])
        self.V.tensor_copy(hif[:], hi[:])
        self.V.tensor_tensor(hif[:], wrf[:], hif[:], ALU.subtract)
        self.V.tensor_copy(lo[:], hif[:])
        c.w(nc.tensor, "s_dve")
        for k in range(NCH):
            nc.tensor.matmul(self.ps_a[0:1, 0:8], self.onec_bf[:], hi[:, k * 8:(k + 1) * 8],
                             start=(k == 0), stop=False)
            nc.tensor.matmul(self.ps_a[0:1, 0:8], self.onec_bf[:], lo[:, k * 8:(k + 1) * 8],
                             start=False, stop=(k == NCH - 1)).then_inc(*c.inc("s_mm"))
        c.w(nc.vector, "s_mm")
        self.V.tensor_copy(csrow[:], self.ps_a[0:1, 0:8])
        c.w(nc.tensor, "s_dve")
        nc.tensor.matmul(self.ps_b[:, 0:8], self.ones_bf[0:1, :], csrow[0:1, :],
                         start=True, stop=True).then_inc(*c.inc("s_mm"))
        c.w(nc.vector, "s_mm")
        self.V.tensor_copy(csb[:], self.ps_b[:, 0:8])
        if DBG_DUMP:
            dmp = nc.dram_tensor("dbg_prep_" + tag, [128, 136], F32, kind="ExternalOutput").ap()
            dsb = c.E(nc.sbuf_tensor("dbg_prep_sb" + tag, [128, 136], F32))
            self.V.tensor_copy(dsb[:, 0:8], csb[:])
            self.V.tensor_copy(dsb[:, 8:72], hi[:])
            self.V.tensor_copy(dsb[:, 72:136], lo[:])
            c.w(nc.sync, "s_dve")
            nc.sync.dma_start(out=dmp, in_=dsb[:]).then_inc(*c.inc("s_x", 16))



    def a0_decl(self):
        self.small_const("a0_n1c", [128, 8])
        self.small_const("a0_n2c", [128, 8])
        self.small_const("a0_bdec", [128, 8])
        self.small_const("a0_bgat", [128, 8])
        self.small_const("a0_bcc", [128, 8])
        self.small_const("a0_cw", [128, 56])
        self.small_const("a0_ssc", [128, 1])
        self.small_const("a0_msc", [128, 1])
        self.a0_st1 = self.din("a0_st1", [45, 128, 1024])
        self.a0_wpk = (self.din("a0_wu", [1, 8, 128, 4096]),
                       self.din("a0_wg", [1, 8, 128, 4096]),
                       self.din("a0_wd", [1, 8, 128, 4096]))

    def a1_decl(self):
        self.small_const("a1_bg", [128, 8])
        self.small_const("a1_bv", [128, 8])
        self.small_const("a1_bd", [128, 8])
        self.a1_st = self.din("a1_st", [24, 128, 1024])

    def wpiece(self, src3d, j, dst_bf_cols, pe_gate=None):
        nc, c = self.nc, self.c
        for h in range(2):
            sj = self.wj
            slot = self.wstage2[:, (sj % 2) * 512:(sj % 2) * 512 + 512]
            if sj >= 2:
                c.w(nc.sync, "s_pool", self.wc0 + sj - 1)
            nc.sync.dma_start(out=slot, in_=src3d[j][:, h * 512:(h + 1) * 512]).then_inc(*c.inc("s_w", 16))
            c.w(nc.gpsimd, "s_w", 16 * (sj + 1))
            if pe_gate is not None and h == 0:
                c.w(nc.gpsimd, "s_mm", pe_gate)
            self.P.tensor_copy(dst_bf_cols[:, h * 512:(h + 1) * 512], slot)
            self.wj += 1
        return self.wc0 + self.wj

    def wreset(self):
        self.wj = 0
        self.wc0 = self.c.n["s_pool"]

    def a0_tile(self, x3v, main, h2n_3d=None):
        nc, c = self.nc, self.c
        S = self.smalls
        n = AT
        self.wreset()
        c.w(nc.vector, "s_mm")
        xin3 = self.d3(self.a_xin[:, 0:NCH * n], n)
        self.V.tensor_copy(xin3, x3v)
        c.w(nc.scalar, "s_dve")
        xsl = lambda k: self.a_xin[:, k * n:(k + 1) * n]
        self.stats(xsl, n, False, self.cst[0:1, 0:1], self.arows_f(), self.arows_b(),
                   self.axb_bf, self.axq_bf)
        self.stat_bcast(n, self.arows_b())
        self.V.tensor_tensor(xin3, xin3, self.bc3(self.ps_c[:, 0:n], n), ALU.subtract)
        self.V.tensor_tensor(xin3, xin3, self.bc3(self.ps_d[:, 0:n], n),
                                ALU.mult)
        if main:
            for m in range(NCH):
                self.V.tensor_copy(self.a_h[:, m * (n + 6):m * (n + 6) + 6],
                                      self.a_tail[:, m * 6:(m + 1) * 6])
        self.stats(lambda k: self.a_xin[:, k * n:(k + 1) * n], n, True, self.cst[0:1, 1:2],
                   self.arows_f(), self.arows_b(), self.axb_bf, self.axq_bf)
        self.stat_bcast(n, self.arows_b(), mean_too=False)
        for k in range(NCH):
            hk = self.a_h[:, k * (n + 6) + 6:k * (n + 6) + 6 + n]
            self.V.scalar_tensor_tensor(hk, self.a_xin[:, k * n:(k + 1) * n],
                                           S["a0_n1c"][:, k:k + 1], self.ps_d[:, 0:n],
                                           ALU.mult, ALU.mult)
            self.V.tensor_copy(self.a_hb[:, k * n:(k + 1) * n], hk)
        v_h = c.n["s_dve"]
        for sec, bias_c in ((0, "a0_bdec"), (1, "a0_bgat")):
            first_act = True
            for half in range(2):
                gate_v = c.n["s_mm"]
                for piece in range(4):
                    self.wpiece(self.a0_st1, sec * 8 + half * 4 + piece,
                                self.wslot[:, piece * 1024:(piece + 1) * 1024],
                                pe_gate=(gate_v if piece == 0 and (sec > 0 or half > 0) else None))
                c.w(nc.tensor, "s_pool", self.wc0 + self.wj)
                c.w(nc.tensor, "s_dve", v_h)
                for ml in range(4):
                    m = half * 4 + ml
                    pp = self.ps_a if m % 2 == 0 else self.ps_b
                    if m >= 2:
                        c.w(nc.tensor, "s_act", c.n["s_act"] - 1)
                    for k in range(NCH):
                        mm = nc.tensor.matmul(pp[:, 0:n],
                                              self.wslot[:, k * 512 + ml * 128:k * 512 + ml * 128 + 128],
                                              self.a_hb[:, k * n:(k + 1) * n],
                                              start=(k == 0), stop=(k == NCH - 1))
                    mm.then_inc(*c.inc("s_mm"))
                    c.w(nc.scalar, "s_mm")
                    if sec == 1 and first_act:
                        c.w(nc.scalar, "s_dve")
                        first_act = False
                    self.A.activation(self.a_A[:, m * n:(m + 1) * n], pp[:, 0:n], AF.Sigmoid,
                                         bias=S[bias_c][:, m:m + 1])
            if sec == 0:
                c.w(nc.vector, "s_act")
                for m in range(NCH):
                    am = self.a_A[:, m * n:(m + 1) * n]
                    self.V.tensor_scalar(am, am, 0.05, 0.95, ALU.max, ALU.min)
                c.w(nc.scalar, "s_dve")
                for m in range(NCH):
                    self.A.activation(self.a_A[:, m * n:(m + 1) * n],
                                         self.a_A[:, m * n:(m + 1) * n], AF.Ln)
                c.w(nc.vector, "s_act")
                for m in range(NCH):
                    self.V.tensor_copy(self.a_lc[:, m * (n + 1):m * (n + 1) + 1],
                                          self.a_carry[:, m:m + 1])
                    self.V.tensor_tensor_scan(self.a_lc[:, m * (n + 1) + 1:m * (n + 1) + 1 + n],
                                                 self.ones128[:, 0:n], self.a_A[:, m * n:(m + 1) * n],
                                                 self.a_carry[:, m:m + 1], ALU.mult, ALU.add)
                    self.V.tensor_copy(self.a_carry[:, m:m + 1],
                                          self.a_lc[:, m * (n + 1) + n:m * (n + 1) + n + 1])
        c.w(nc.vector, "s_act")
        for m in range(NCH):
            n1 = n + 1
            am = self.a_A[:, m * n:(m + 1) * n]
            hk = self.a_h[:, m * (n + 6) + 6:m * (n + 6) + 6 + n]
            lcsh = self.a_lc[:, m * n1:m * n1 + n]
            ofs = self.a_ofs[:, m:m + 1] if main else self.zcol[:]
            self.V.tensor_tensor(am, am, hk, ALU.mult)
            self.V.tensor_scalar(self.a_D[:, m * n:(m + 1) * n], lcsh, ofs, LN1E7,
                                    ALU.add, ALU.max)
            c.w(nc.scalar, "s_dve")
            self.A.activation(self.a_D[:, m * n:(m + 1) * n], self.a_D[:, m * n:(m + 1) * n],
                                 AF.Exp, scale=-1.0)
            c.w(nc.vector, "s_act")
            self.V.tensor_tensor(self.a_D[:, m * n:(m + 1) * n], am,
                                    self.a_D[:, m * n:(m + 1) * n], ALU.mult)
            if main:
                c.w(nc.scalar, "s_dve")
                self.A.activation(am, self.a_lc[:, m * n1 + 1:m * n1 + 1 + n],
                                     AF.Exp, bias=self.a_ofs[:, m:m + 1])
            self.V.tensor_tensor_scan(self.a_D[:, m * n:(m + 1) * n], self.ones128[:, 0:n],
                                         self.a_D[:, m * n:(m + 1) * n],
                                         self.a_carry[:, 8 + m:9 + m], ALU.mult, ALU.add)
            self.V.tensor_copy(self.a_carry[:, 8 + m:9 + m],
                                  self.a_D[:, m * n + n - 1:m * n + n])
            if main:
                c.w(nc.vector, "s_act")
                self.V.scalar_tensor_tensor(self.a_cat[:, (8 + m) * n:(9 + m) * n],
                                               am,
                                               S["a0_ssc"][:, 0:1], self.a_D[:, m * n:(m + 1) * n],
                                               ALU.mult, ALU.mult)
        if not main:
            for m in range(NCH):
                self.V.tensor_copy(self.a_tail[:, m * 6:(m + 1) * 6],
                                      self.a_h[:, m * (n + 6) + n:m * (n + 6) + n + 6])
            return
        for m in range(NCH):
            he = self.a_h[:, m * (n + 6):m * (n + 6) + n + 6]
            self.V.tensor_scalar(self.a_A[:, m * n:(m + 1) * n], he[:, 0:n],
                                    S["a0_cw"][:, 0 * 8 + m:0 * 8 + m + 1], None, ALU.mult)
            for j in range(1, 7):
                dst = self.a_cat[:, m * n:(m + 1) * n] if j == 6 else self.a_A[:, m * n:(m + 1) * n]
                op = self.V.scalar_tensor_tensor(dst, he[:, j:j + n],
                                                    S["a0_cw"][:, j * 8 + m:j * 8 + m + 1],
                                                    self.a_A[:, m * n:(m + 1) * n],
                                                    ALU.mult, ALU.add)
            op
        for piece in range(2):
            self.wpiece(self.a0_st1, 16 + piece, self.wslot[:, piece * 1024:(piece + 1) * 1024],
                        pe_gate=(c.n["s_mm"] if piece == 0 else None))
        c.w(nc.tensor, "s_pool", self.wc0 + self.wj)
        c.w(nc.tensor, "s_act")
        c.w(nc.tensor, "s_dve")
        for c2 in range(2):
            pp = self.ps_a if c2 % 2 == 0 else self.ps_b
            for k in range(NCH):
                mm = nc.tensor.matmul(pp[:, 0:n],
                                      self.wslot[:, k * 256 + c2 * 128:k * 256 + c2 * 128 + 128],
                                      self.a_hb[:, k * n:(k + 1) * n], start=(k == 0), stop=(k == NCH - 1))
            mm.then_inc(*c.inc("s_mm"))
            c.w(nc.scalar, "s_mm")
            self.A.copy(self.kx_bf[:, c2 * n:(c2 + 1) * n], pp[:, 0:n])
        self.wpiece(self.a0_st1, 18, self.wslot[:, 2048:3072],
                    pe_gate=c.n["s_mm"])
        c.w(nc.tensor, "s_pool", self.wc0 + self.wj)
        c.w(nc.tensor, "s_act")
        for c2 in range(2):
            pp = self.ps_c if c2 % 2 == 0 else self.ps_d
            for k in range(2):
                mm = nc.tensor.matmul(pp[:, 0:n],
                                      self.wslot[:, 2048 + k * 256 + c2 * 128:2048 + k * 256 + c2 * 128 + 128],
                                      self.kx_bf[:, k * n:(k + 1) * n], start=(k == 0), stop=(k == 1))
            mm.then_inc(*c.inc("s_mm"))
            c.w(nc.scalar, "s_mm")
            self.A.copy(self.mm_bf[:, c2 * n:(c2 + 1) * n], pp[:, 0:n])
        for piece in range(2):
            self.wpiece(self.a0_st1, 19 + piece, self.wslot[:, piece * 1024:(piece + 1) * 1024],
                        pe_gate=(c.n["s_mm"] if piece == 0 else None))
        c.w(nc.tensor, "s_pool", self.wc0 + self.wj)
        c.w(nc.tensor, "s_act")
        for m in range(NCH):
            pp = self.ps_a if m % 2 == 0 else self.ps_b
            if m >= 2:
                c.w(nc.tensor, "s_dve", c.n["s_dve"] - 1)
            for k in range(2):
                mm = nc.tensor.matmul(pp[:, 0:n],
                                      self.wslot[:, k * 1024 + m * 128:k * 1024 + m * 128 + 128],
                                      self.mm_bf[:, k * n:(k + 1) * n], start=(k == 0), stop=(k == 1))
            mm.then_inc(*c.inc("s_mm"))
            c.w(nc.vector, "s_mm")
            self.V.tensor_scalar(self.a_cat[:, (16 + m) * n:(17 + m) * n], pp[:, 0:n],
                                    S["a0_msc"][:, 0:1], None, ALU.mult)
        banks = [self.ps_a[:, 0:n], self.ps_b[:, 0:n], self.ps_c[:, 0:n], self.ps_d[:, 0:n],
                 self.ps_big[:, 0:n], self.ps_big[:, 512:512 + n],
                 self.ps_big[:, 1024:1024 + n], self.ps_big[:, 1536:1536 + n]]
        c.w(nc.tensor, "s_dve")
        c.w(nc.tensor, "s_act")
        wc_base = c.n["s_mm"]
        for kc in range(24):
            self.wpiece(self.a0_st1, 21 + kc, self.wc_ring[:, (kc % 2) * 1024:(kc % 2) * 1024 + 1024],
                        pe_gate=(wc_base + kc - 1 if kc >= 2 else None))
            c.w(nc.tensor, "s_pool", self.wc0 + self.wj)
            for m in range(NCH):
                mm = nc.tensor.matmul(banks[m],
                                      self.wc_ring[:, (kc % 2) * 1024 + m * 128:(kc % 2) * 1024 + m * 128 + 128],
                                      self.a_cat[:, kc * n:(kc + 1) * n],
                                      start=(kc == 0), stop=(kc == 23))
            mm.then_inc(*c.inc("s_mm"))
        c.w(nc.vector, "s_mm")
        for m in range(NCH):
            self.V.scalar_tensor_tensor(self.a_xin[:, m * n:(m + 1) * n], banks[m],
                                           S["a0_bcc"][:, m:m + 1], self.a_xin[:, m * n:(m + 1) * n],
                                           ALU.add, ALU.add)
        xin3 = self.d3(self.a_xin[:, 0:NCH * n], n)
        self.V.tensor_tensor(x3v, x3v, xin3, ALU.add)
        self.stats(lambda k: self.a_xin[:, k * n:(k + 1) * n], n, True, self.cst[0:1, 1:2],
                   self.arows_f(), self.arows_b(), self.axb_bf, self.axq_bf)
        self.stat_bcast(n, self.arows_b(), mean_too=False)
        for k in range(NCH):
            self.V.scalar_tensor_tensor(self.a_D[:, k * n:(k + 1) * n], self.a_xin[:, k * n:(k + 1) * n],
                                           S["a0_n2c"][:, k:k + 1], self.ps_d[:, 0:n],
                                           ALU.mult, ALU.mult)
        self.V.tensor_copy(h2n_3d, self.d3(self.a_D[:, 0:NCH * n], n))
        for m in range(NCH):
            self.V.tensor_copy(self.a_tail[:, m * 6:(m + 1) * 6],
                                  self.a_h[:, m * (n + 6) + n:m * (n + 6) + n + 6])

    def arows_f(self):
        return self.arows

    def arows_b(self):
        return self.pb[0:1, 12800:12800 + 4 * AT]

    def attn0(self):
        nc, c = self.nc, self.c
        ntile = self.tok // AT
        self.V.memset(self.a_carry[:, 0:16], 0.0)
        c.end_body()
        with RegScope(nc):
            with nc.Fori(0, ntile) as a_sv:
                self.a0_tile(self.x3a(a_sv, AT), main=False)
                c.end_body()
        self.V.tensor_copy(self.gath[:, 0:8], self.a_carry[:, 0:8])
        self.V.tensor_copy(self.gath[:, 8:16], self.a_carry[:, 8:16])
        self.V.tensor_copy(self.gath[:, 16:64], self.a_tail[:, 0:48])
        c.w(nc.gpsimd, "s_dve")
        nc.gpsimd.dma_start(out=self.cc1_in.ap(), in_=self.gath[:, 0:64]).then_inc(*c.inc("s_boot", 16))
        c.w(nc.gpsimd, "s_boot")
        rg = [[2 * i, 2 * i + 1] for i in range(self.n_pairs)]
        nc.gpsimd.collective_compute("AllGather", ALU.bypass, replica_groups=rg,
                                     ins=[self.cc1_in.ap().opt()],
                                     outs=[self.cc1_out.ap().opt()]).then_inc(*c.inc("s_cc"))
        c.w(nc.gpsimd, "s_cc")
        nc.gpsimd.dma_start(out=self.gath[:, 0:64], in_=self.cc1_out.ap()[0:128, :]).then_inc(*c.inc("s_boot", 16))
        c.w(nc.vector, "s_boot")
        self.V.tensor_scalar(self.a_ofs[:, 0:8], self.gath[:, 0:8], self.mask[:, 0:1], None, ALU.mult)
        self.V.tensor_scalar(self.a_ofs[:, 8:16], self.gath[:, 8:16], self.mask[:, 0:1], None, ALU.mult)
        self.V.memset(self.a_carry[:, 0:8], 0.0)
        self.V.tensor_copy(self.a_carry[:, 8:16], self.a_ofs[:, 8:16])
        self.V.tensor_scalar(self.a_tail[:, 0:48], self.gath[:, 16:64],
                                self.mask[:, 0:1], None, ALU.mult)
        c.end_body()
        ngrp = self.ngrp
        tpg = self.grp // AT
        with RegScope(nc):
            with nc.Fori(0, ngrp) as g_sv:
                with nc.Fori(0, tpg) as t_sv:
                    h2n_3d = self.d3(self.xn[:, 0:NCH * self.grp], self.grp)[:, :, ts(t_sv, AT)]
                    self.a0_tile(self.x3t(g_sv, t_sv, AT), main=True, h2n_3d=h2n_3d)
                    c.end_body()
                self.moe_units(g_sv, self.a0_wpk, None, None, 1, 8, 1.0, False, False)

    def a1_tile(self, x3v, main):
        nc, c = self.nc, self.c
        S = self.smalls
        n = AT
        self.wreset()
        c.w(nc.vector, "s_mm")
        xin3 = self.d3(self.a_xin[:, 0:NCH * n], n)
        self.V.tensor_copy(xin3, x3v)
        c.w(nc.scalar, "s_dve")
        xsl = lambda k: self.a_xin[:, k * n:(k + 1) * n]
        self.stats(xsl, n, False, self.cst[0:1, 0:1], self.arows_f(), self.arows_b(),
                   self.axb_bf, self.axq_bf)
        self.stat_bcast(n, self.arows_b())
        self.V.tensor_tensor(xin3, xin3, self.bc3(self.ps_c[:, 0:n], n), ALU.subtract)
        self.V.tensor_tensor(xin3, xin3, self.bc3(self.ps_d[:, 0:n], n), ALU.mult)
        self.V.tensor_copy(self.d3(self.a_hb[:, 0:NCH * n], n), xin3)
        v_h = c.n["s_dve"]
        for sec, bias_c, func, dst in ((0, "a1_bg", AF.Sigmoid, self.a_h),
                                       (1, "a1_bv", AF.Tanh, self.a_A),
                                       (2, "a1_bd", AF.Sigmoid, self.a_D)):
            for half in range(2):
                gate_v = c.n["s_mm"]
                for piece in range(4):
                    self.wpiece(self.a1_st, sec * 8 + half * 4 + piece,
                                self.wslot[:, piece * 1024:(piece + 1) * 1024],
                                pe_gate=(gate_v if piece == 0 and (sec > 0 or half > 0) else None))
                c.w(nc.tensor, "s_pool", self.wc0 + self.wj)
                c.w(nc.tensor, "s_dve", v_h)
                for ml in range(4):
                    m = half * 4 + ml
                    pp = self.ps_a if m % 2 == 0 else self.ps_b
                    if m >= 2:
                        c.w(nc.tensor, "s_act", c.n["s_act"] - 1)
                    for k in range(NCH):
                        mm = nc.tensor.matmul(pp[:, 0:n],
                                              self.wslot[:, k * 512 + ml * 128:k * 512 + ml * 128 + 128],
                                              self.a_hb[:, k * n:(k + 1) * n], start=(k == 0), stop=(k == NCH - 1))
                    mm.then_inc(*c.inc("s_mm"))
                    c.w(nc.scalar, "s_mm")
                    self.A.activation(dst[:, m * n:(m + 1) * n] if dst is not self.a_h
                                         else self.a_h[:, m * (n + 6) + 6:m * (n + 6) + 6 + n],
                                         pp[:, 0:n], func, bias=S[bias_c][:, m:m + 1])
        c.w(nc.vector, "s_act")
        for m in range(NCH):
            gm = self.a_h[:, m * (n + 6) + 6:m * (n + 6) + 6 + n]
            self.V.tensor_tensor(self.a_A[:, m * n:(m + 1) * n], gm,
                                    self.a_A[:, m * n:(m + 1) * n], ALU.mult)
            self.V.tensor_scalar(self.a_D[:, m * n:(m + 1) * n], self.a_D[:, m * n:(m + 1) * n],
                                    0.998, 0.001, ALU.mult, ALU.add)
            self.V.tensor_tensor_scan(self.a_A[:, m * n:(m + 1) * n], self.a_D[:, m * n:(m + 1) * n],
                                         self.a_A[:, m * n:(m + 1) * n],
                                         self.a_carry[:, m:m + 1], ALU.mult, ALU.add)
            self.V.tensor_copy(self.a_carry[:, m:m + 1],
                                  self.a_A[:, m * n + n - 1:m * n + n])

    def attn1(self):
        nc, c = self.nc, self.c
        ntile = self.tok // AT
        self.V.memset(self.a_carry[:, 0:8], 0.0)
        c.end_body()
        with RegScope(nc):
            with nc.Fori(0, ntile) as a_sv:
                self.a1_tile(self.x3a(a_sv, AT), main=False)
                c.end_body()
        self.V.tensor_copy(self.gath2[:, 0:8], self.a_carry[:, 0:8])
        c.w(nc.gpsimd, "s_dve")
        nc.gpsimd.dma_start(out=self.cc2_in.ap(), in_=self.gath2[:, 0:8]).then_inc(*c.inc("s_boot", 16))
        c.w(nc.gpsimd, "s_boot")
        rg = [[2 * i, 2 * i + 1] for i in range(self.n_pairs)]
        nc.gpsimd.collective_compute("AllGather", ALU.bypass, replica_groups=rg,
                                     ins=[self.cc2_in.ap().opt()],
                                     outs=[self.cc2_out.ap().opt()]).then_inc(*c.inc("s_cc2"))
        c.w(nc.gpsimd, "s_cc2")
        nc.gpsimd.dma_start(out=self.gath2[:, 0:8], in_=self.cc2_out.ap()[0:128, :]).then_inc(*c.inc("s_boot", 16))
        c.w(nc.vector, "s_boot")
        self.V.tensor_scalar(self.a_carry[:, 0:8], self.gath2[:, 0:8],
                                self.mask[:, 0:1], None, ALU.mult)
        c.end_body()
        with RegScope(nc):
            with nc.Fori(0, ntile) as a_sv:
                self.a1_tile(self.x3a(a_sv, AT), main=True)
                x3v = self.x3a(a_sv, AT)
                c.w(nc.vector, "s_dve")
                self.V.tensor_tensor(x3v, x3v, self.d3(self.a_A[:, 0:NCH * AT], AT),
                                        ALU.add)
                c.end_body()



def build_program(tok, parts, n_pairs=4):
    nc = bass.Bass("TRN2", target_bir_lowering=False)
    ctx = ExitStack()
    with ctx:
        b = Build(nc, ctx, tok=tok, n_pairs=n_pairs)
        b.prologue()
        if "moe1a" in parts or "moe1b" in parts:
            b.moe_decl_smalls("l0f", "l0")
        if "moe2a" in parts or "moe2b" in parts:
            b.moe_decl_smalls("l1f", "l1")
        if "attn0" in parts:
            b.a0_decl()
        if "attn1" in parts:
            b.a1_decl()
        if "moe1a" in parts or "moe1b" in parts:
            b.moe_decl("l0f", "l0", 8, 8)
        if "moe2a" in parts or "moe2b" in parts:
            b.moe_decl("l1f", "l1", 8, 4)
        b.prologue_finish()
        if "moe1a" in parts or "moe1b" in parts:
            b.moe_router_prep("l0f")
        if "moe2a" in parts or "moe2b" in parts:
            b.moe_router_prep("l1f")
        b.c.end_body()
        for p in parts:
            if p in ("moe1a", "moe1b"):
                b.moe_run("l0f", "l0", 8, 8)
            elif p in ("moe2a", "moe2b"):
                b.moe_run("l1f", "l1", 8, 4)
            elif p == "attn0":
                b.attn0()
            elif p == "attn1":
                b.attn1()
        b.epilogue()
    return nc, b.inputs



def host_packs(I):
    P = {}
    WU0 = np.concatenate([I["L0_ffn_Wu_s"][None], I["L0_ffn_Wu"]], 0)
    WG0 = np.concatenate([I["L0_ffn_Wg_s"][None], I["L0_ffn_Wg"]], 0)
    WD0 = np.concatenate([I["L0_ffn_Wd_s"][None], I["L0_ffn_Wd"]], 0)
    P["l0f_wu"], P["l0f_wg"], P["l0f_wd"] = pack_moe(WU0, WG0, WD0)
    sel = np.zeros((8, 8, 128), np.float32)
    for e in range(8):
        sel[e, e, :] = 1.0
    P["l0f_selp"] = sel
    P["l0f_wrf"] = pack_kchunks(I["L0_ffn_Wr"])
    Wl1 = np.asarray(I["L1_ffn_Wl1"], np.float32)
    P["l1f_wu"], P["l1f_wg"], P["l1f_wd"] = pack_moe(Wl1[:, :, 2048:], Wl1[:, :, :2048], _f(I["L1_ffn_Wl2"]))
    P["l1f_selp"] = sel
    P["l1f_wrf"] = pack_kchunks(_f(I["L1_ffn_Wr"]).T)
    P["l1f_brc"] = _f(np.broadcast_to(_f(I["L1_ffn_br"])[None, :], (128, 8)))
    P["l1f_bnc"] = _f(np.broadcast_to(_f(I["L1_ffn_biases_N"])[None, :], (128, 8)))
    P["l1f_bl2"] = _f(I["L1_ffn_bl2"])
    bl1 = _f(I["L1_ffn_bl1"])
    blA = bl1[:, :2048].reshape(8, 4, 4, 128).transpose(0, 1, 3, 2)
    blB = bl1[:, 2048:].reshape(8, 4, 4, 128).transpose(0, 1, 3, 2)
    P["l1f_blup"] = _f(np.concatenate([blA, blB], axis=-1))
    P["a0_n1c"] = pack_cols(I["L0_attn_norm1_w"])
    P["a0_n2c"] = pack_cols(I["L0_attn_norm2_w"])
    P["a0_bdec"] = pack_cols(I["L0_attn_bdecay"])
    P["a0_bgat"] = pack_cols(I["L0_attn_bgate"])
    P["a0_bcc"] = pack_cols(I["L0_attn_bc"])
    P["a0_cw"] = _f(_f(I["L0_attn_conv_w"]).reshape(8, 128, 7).transpose(1, 2, 0).reshape(128, 56))
    P["a0_ssc"] = _f(np.broadcast_to(np.float32(I["L0_attn_state_scale"]), (128, 1)))
    P["a0_msc"] = _f(np.broadcast_to(np.float32(I["L0_attn_mem_scale"]), (128, 1)))
    P["a0_st1"] = pack_attn0_stream(_f(I["L0_attn_Wdecay"]), _f(I["L0_attn_Wgate"]),
                                    _f(I["L0_attn_Wk"]), _f(I["L0_attn_M"]).T,
                                    _f(I["L0_attn_Wmem"]), _f(I["L0_attn_Wc"]))
    P["a0_wu"], P["a0_wg"], P["a0_wd"] = pack_moe(_f(I["L0_attn_Wup"])[None], _f(I["L0_attn_Wg"])[None],
                                                  _f(I["L0_attn_Wdown"])[None])
    P["a1_bg"] = pack_cols(I["L1_attn_bg"])
    P["a1_bv"] = pack_cols(I["L1_attn_bv"])
    P["a1_bd"] = pack_cols(I["L1_attn_bd"])
    P["a1_st"] = pack_attn1_stream(_f(I["L1_attn_Wg"]).T, _f(I["L1_attn_Wv"]).T,
                                   _f(I["L1_attn_Wd"]).T)
    P["ident_in"] = np.eye(128, dtype=np.float32)
    return P


LAUNCHES = [ALL_PARTS]


def run_launches(x, packs, launches, tok, n_pairs, extra_run_kw=None):
    cur = x
    n_cores = 2 * n_pairs
    for parts in launches:
        nc, innames = build_program(tok, parts, n_pairs=n_pairs)
        in_maps = []
        for cix in range(n_cores):
            m = {}
            for nm in innames:
                if nm == "x_in":
                    m[nm] = cur[cix]
                elif nm == "mask_in":
                    m[nm] = np.full((128, 1), float(cix % 2), np.float32)
                else:
                    m[nm] = packs[nm]
            in_maps.append(m)
        res = run_bass_kernel_spmd(nc, in_maps, core_ids=list(range(n_cores)),
                                   **(extra_run_kw or {}))
        cur = [np.ascontiguousarray(res.results[cix]["x_out"]) for cix in range(n_cores)]
    return cur


def kernel(**inputs):
    x = np.asarray(inputs["L0_attn_x"], np.float32)
    B, T, _ = x.shape
    half = T // 2
    packs = host_packs(inputs)
    cur = [np.ascontiguousarray(x[c // 2, (c % 2) * half:(c % 2 + 1) * half, :].T)
           for c in range(8)]
    cur = run_launches(cur, packs, LAUNCHES, tok=half, n_pairs=4)
    out = np.empty((B, T, D), np.float32)
    for c in range(8):
        out[c // 2, (c % 2) * half:(c % 2 + 1) * half, :] = cur[c].T
    return out
